# version 09 (final)
# baseline (speedup 1.0000x reference)
import numpy as np
import ml_dtypes
from contextlib import ExitStack

import concourse.bass as bass
import concourse.mybir as mybir
from concourse.bass_utils import run_bass_kernel_spmd

F32 = mybir.dt.float32
BF16 = mybir.dt.bfloat16
AF = mybir.ActivationFunctionType
ALU = mybir.AluOpType
AX = mybir.AxisListType
BF = ml_dtypes.bfloat16

HID = 2048
HIDDEN = 5632
NH, NKV, D = 16, 4, 128
NTOK, NKTOK = 1024, 1536
MC, HC = 16, 44
SCALE = float(D) ** -0.5
MAGIC32 = 12582912.0
DMAGIC = float(np.float32(6755399441055744.0 - 12582912.0))
MEANMUL = float(np.float32(8.66976677e-08))
INV2048 = float(np.float32(1.0 / 2048.0))
DEBUG = False
ENGINES = ("sync", "vector", "scalar", "tensor", "gpsimd")
V, S, TE, GY, SY = "vector", "scalar", "tensor", "gpsimd", "sync"


class Prog:
    def __init__(self, nc, ctx):
        self.nc, self.ctx = nc, ctx
        self.q = {e: [] for e in ENGINES}
        self.sems, self.cnt = {}, {}

    def sem(self, name):
        self.sems[name] = self.ctx.enter_context(self.nc.semaphore(name))
        self.cnt[name] = 0

    def dma(self, eng, out, in_, sem):
        h = self.sems[sem]
        self.cnt[sem] += 16
        self.q[eng].append(lambda e, out=out, in_=in_, h=h:
                           e.dma_start(out=out, in_=in_).then_inc(h, 16))
        return self.cnt[sem]

    def dmat(self, eng, out, in_, sem):
        h = self.sems[sem]
        self.cnt[sem] += 16
        self.q[eng].append(lambda e, out=out, in_=in_, h=h:
                           e.dma_start_transpose(out, in_).then_inc(h, 16))
        return self.cnt[sem]

    def op(self, eng, meth, *args, sem=None, **kw):
        h = self.sems[sem] if sem else None
        if sem:
            self.cnt[sem] += 1

        def fn(e, meth=meth, args=args, kw=kw, h=h):
            i = getattr(e, meth)(*args, **kw)
            if h is not None:
                i.then_inc(h, 1)
        self.q[eng].append(fn)
        return self.cnt[sem] if sem else None

    def wait(self, eng, sem, val):
        if not val:
            return
        h = self.sems[sem]
        self.q[eng].append(lambda e, h=h, val=val: e.wait_ge(h, val))

    def opf(self, eng, meth, *args, **kw):
        sem = "VF" if eng == V else "SF"
        n = self.op(eng, meth, *args, sem=sem, **kw)
        self.wait(eng, sem, n)
        return n

    def gate(self, engines=ENGINES):
        snap = [(s, c) for s, c in self.cnt.items() if c > 0]
        for e in engines:
            for s, v in snap:
                self.wait(e, s, v)


def hb(ap2, nh, n):
    return ap2.unsqueeze(1).broadcast_to([128, nh, n])


def build_nc():
    nc = bass.Bass("TRN2", target_bir_lowering=False)
    A = lambda h: h.ap()
    dt = nc.dram_tensor

    x_own = A(dt("x_own", [NTOK, HID], F32, kind="ExternalInput"))
    x_halo = A(dt("x_halo", [512, HID], F32, kind="ExternalInput"))
    wq = A(dt("wq", [HID, HID], F32, kind="ExternalInput"))
    wk = A(dt("wk", [HID, 512], F32, kind="ExternalInput"))
    wv = A(dt("wv", [HID, 512], F32, kind="ExternalInput"))
    wo = A(dt("wo", [HID, HID], F32, kind="ExternalInput"))
    w1t = A(dt("w1t", [HID, HIDDEN], F32, kind="ExternalInput"))
    w3t = A(dt("w3t", [HID, HIDDEN], F32, kind="ExternalInput"))
    w2t = A(dt("w2t", [HIDDEN, HID], F32, kind="ExternalInput"))
    w1c = A(dt("w1c", [96, 131072], F32, kind="ExternalInput"))
    w2c = A(dt("w2c", [96, 131072], F32, kind="ExternalInput"))
    w3c = A(dt("w3c", [96, 131072], F32, kind="ExternalInput"))
    masks_i = A(dt("masks", [128, 40 * 128], BF16, kind="ExternalInput"))
    ropes = A(dt("ropes", [128, 4 * 12 * 128], F32, kind="ExternalInput"))
    consts = A(dt("consts", [128, 16], F32, kind="ExternalInput"))
    ones_bf = A(dt("ones_bf", [128, 1], BF16, kind="ExternalInput"))
    ones_f1 = A(dt("ones_f1", [1, 128], F32, kind="ExternalInput"))

    out = A(dt("out", [NTOK, HID], F32, kind="ExternalOutput"))
    h_scr = A(dt("h_scr", [NTOK, HIDDEN], F32, kind="Internal"))
    t2_scr = A(dt("t2_scr", [HIDDEN, HID], BF16, kind="Internal"))
    xq_scr = A(dt("xq_scr", [NTOK, HID], BF16, kind="Internal"))
    xatt_scr = A(dt("xatt_scr", [NTOK, HID], F32, kind="Internal"))
    if DEBUG:
        d_xnT = A(dt("d_xnT", [128, MC * NKTOK], BF16, kind="ExternalOutput"))
        d_stats = A(dt("d_stats", [128, 32], F32, kind="ExternalOutput"))
        d_wsc = A(dt("d_wsc", [128, 6], F32, kind="ExternalOutput"))
        d_fin = A(dt("d_fin", [1, 48], F32, kind="ExternalOutput"))
        d_s2red = A(dt("d_s2red", [12, 3], F32, kind="ExternalOutput"))
        d_qhT = A(dt("d_qhT", [128, NH * NTOK], BF16, kind="ExternalOutput"))
        d_khT = A(dt("d_khT", [128, NKV * NKTOK], BF16, kind="ExternalOutput"))
        d_v = A(dt("d_v", [128, 12 * 512], BF16, kind="ExternalOutput"))
        d_avn = A(dt("d_avn", [128, 8 * NKV * 512], BF16, kind="ExternalOutput"))
        d_xatt = A(dt("d_xatt", [NTOK, HID], F32, kind="ExternalOutput"))
        d_t1g = A(dt("d_t1g", [128, MC * 512], BF16, kind="ExternalOutput"))
        d_t3g = A(dt("d_t3g", [128, MC * 512], BF16, kind="ExternalOutput"))
        d_xqT = A(dt("d_xqT", [128, MC * NTOK], BF16, kind="ExternalOutput"))
        d_xq = A(dt("d_xq", [NTOK, HID], BF16, kind="ExternalOutput"))
        d_hscr = A(dt("d_hscr", [NTOK, HIDDEN], F32, kind="ExternalOutput"))
        d_t2 = A(dt("d_t2", [HIDDEN, HID], BF16, kind="ExternalOutput"))
        d_hamax = A(dt("d_hamax", [128, 32], F32, kind="ExternalOutput"))
        d_hqT = A(dt("d_hqT", [128, 2 * HC * 512], BF16, kind="ExternalOutput"))

    x_tiled = x_own.rearrange("(t p) h -> t p h", p=128)
    xh_tiled = x_halo.rearrange("(t p) h -> t p h", p=128)
    out_tiled = out.rearrange("(t p) h -> t p h", p=128)
    rope4 = ropes.rearrange("p (a b c) -> p a b c", a=4, b=12)

    ctx = ExitStack()
    sbo = lambda n, s, d=F32, **kw: ctx.enter_context(nc.sbuf_tensor(n, s, d, **kw))

    ct = sbo("ct", [128, 16])
    m_ap = sbo("m_ap", [128, 1])
    zf = sbo("zf", [128, 1])
    stats = sbo("stats", [128, 8, 4])
    hamax = sbo("hamax", [128, 8, 4])
    hpart = sbo("hpart", [128, 8, 12])
    wsc = sbo("wsc", [128, 3])
    wscr = sbo("wscr", [128, 3])
    onesb = sbo("onesb", [128, 1], BF16)
    onesf = sbo("onesf", [1, 128])
    sc_mean = ExitStack()
    sbm = lambda n, s, d=F32, **kw: sc_mean.enter_context(
        nc.sbuf_tensor(n, s, d, **kw))
    s1b = sbm("s1b", [96, 3, 128])
    s2b = sbm("s2b", [12, 3, 1024])
    s2red = sbm("s2red", [12, 3, 1])
    fin = sbm("fin", [1, 3, 16])
    finr = sbm("finr", [1, 3, 4])
    chain_t = sbm("chain_t", [96, 2, 2048])

    p = Prog(nc, ctx)
    for s in ("LD", "CHL", "XB", "GL", "MV", "SC", "MM", "HS", "MR",
              "OD", "FL", "FG", "HL", "TG", "GQ", "VF", "SF"):
        p.sem(s)

    ps = [ctx.enter_context(nc.psum_tensor(f"ps{i}", [128, 512], F32))
          for i in range(8)]

    def mm(bank_ap, lhs, rhs, start, stop, sem=None):
        return p.op(TE, "matmul", bank_ap, lhs, rhs, start=start, stop=stop,
                    sem=sem)

    p.dma(SY, ct[:], consts[:], "LD")
    p.dma(SY, onesb[:], ones_bf[:], "LD")
    n0 = p.dma(SY, onesf[:], ones_f1[:], "LD")
    p.wait(V, "LD", n0)
    p.opf(V, "tensor_scalar", zf[:], ct[:, 0:1], 1.0, 16777216.0, ALU.add,
          ALU.subtract)
    p.op(V, "tensor_scalar", m_ap[:], zf[:], DMAGIC, MAGIC32, ALU.mult,
         ALU.add, sem="MV")

    CHW = 2048
    NCH_PER = 131072 // CHW
    NCHUNK = 3 * NCH_PER
    mats = (w1c, w2c, w3c)
    m_load = [None] * NCHUNK
    m_red = [None] * NCHUNK
    mst = {"l": 0, "r": 0}

    def mean_pump(n):
        for _ in range(n):
            i = mst["l"]
            if i < NCHUNK:
                mi, ci = divmod(i, NCH_PER)
                if i >= 2:
                    p.wait(SY, "MR", m_red[i - 2])
                m_load[i] = p.dma(SY, chain_t[:, i % 2, :],
                                  mats[mi][:, ci * CHW:(ci + 1) * CHW], "CHL")
                mst["l"] = i + 1
            j = mst["r"]
            if j < NCHUNK and m_load[j] is not None:
                mi, cj = divmod(j, NCH_PER)
                nwin = CHW // 1024
                p.wait(V, "CHL", m_load[j])
                m_red[j] = p.op(
                    V, "tensor_reduce",
                    s1b[0:96, mi, cj * nwin:(cj + 1) * nwin],
                    chain_t[0:96, j % 2, :].rearrange("q (n k) -> q n k", k=1024),
                    AX.X, ALU.add, apply_absolute_value=True, sem="MR")
                mst["r"] = j + 1

    mean_pump(2)

    sc_proj = ExitStack()
    sbp = lambda n, s, d=F32, **kw: sc_proj.enter_context(
        nc.sbuf_tensor(n, s, d, **kw))
    xnT = sbp("xnT", [128, MC, NKTOK], BF16, is_transpose_dest=True)
    wproj = sbp("wproj", [128, 2, MC, 512], BF16)
    ropep = sbp("ropep", [128, 2, 12, 128])

    sc_p1 = ExitStack()
    sb1 = lambda n, s, d=F32, **kw: sc_p1.enter_context(
        nc.sbuf_tensor(n, s, d, **kw))
    xt = sb1("xt", [128, 2, HID])
    xnb = sb1("xnb", [128, 2, HID], BF16)
    xnf = sb1("xnf", [128, HID])
    sq_scr = sb1("sq_scr", [128, HID])
    sq_dump = sb1("sq_dump", [128, HID])
    p1s = sb1("p1s", [128, 8])
    xqd = sb1("xqd", [128, 2, HID], BF16)

    p1_mv = [0] * 12
    p1_xb = [0] * 12
    p1_gy = [0] * 12
    for i in range(12):
        own = i < 8
        if i >= 2:
            p.wait(SY, "MV", p1_mv[i - 2])
        src = x_tiled[i] if own else xh_tiled[i - 8]
        ldi = p.dma(SY, xt[:, i % 2, :], src, "LD")
        xtile = xt[:, i % 2, :]

        p.wait(S, "LD", ldi)
        if i >= 2:
            p.wait(S, "MV", p1_mv[i - 2])
        nsq1 = p.op(S, "activation", sq_dump[:], xtile, AF.Square,
                    accum_out=p1s[:, 1:2], sem="SC")

        p.wait(V, "LD", ldi)
        if i >= 2:
            p.wait(V, "XB", p1_xb[i - 2])
            if p1_gy[i - 2]:
                p.wait(V, "GL", p1_gy[i - 2])
        p.opf(V, "tensor_reduce", p1s[:, 0:1], xtile, AX.X, ALU.add)
        p.op(V, "tensor_single_scalar", p1s[:, 2:3], p1s[:, 0:1], INV2048, ALU.mult)
        p.wait(V, "SC", nsq1)
        p.opf(V, "tensor_single_scalar", p1s[:, 3:4], p1s[:, 1:2], INV2048, ALU.mult)
        p.opf(V, "tensor_tensor", p1s[:, 4:5], p1s[:, 2:3], p1s[:, 2:3], ALU.mult)
        nvar = p.op(V, "tensor_tensor", p1s[:, 5:6], p1s[:, 3:4], p1s[:, 4:5],
                    ALU.subtract, sem="MV")
        p.wait(S, "MV", nvar)
        nsq2 = p.op(S, "activation", p1s[:, 6:7], p1s[:, 5:6], AF.Sqrt,
                    bias=ct[:, 4:5], sem="SC")
        p.wait(V, "SC", nsq2)
        p.opf(V, "reciprocal", p1s[:, 7:8], p1s[:, 6:7])
        nxnb = p.op(V, "tensor_scalar", xnb[:, i % 2, :], xtile,
                    p1s[:, 2:3], p1s[:, 7:8], ALU.subtract, ALU.mult, sem="MV")
        if own:
            p.op(V, "tensor_scalar", xnf[:], xtile,
                 p1s[:, 2:3], p1s[:, 7:8], ALU.subtract, ALU.mult)
            p.opf(V, "reduce_max", p1s[:, 0:1], xnf[:], AX.X,
                  apply_absolute_value=True)
            p.opf(V, "tensor_scalar_max", p1s[:, 0:1], p1s[:, 0:1], 1e-5)
            p.opf(V, "reciprocal", p1s[:, 1:2], p1s[:, 0:1])
            p.opf(V, "tensor_single_scalar", stats[:, i, 0:1], p1s[:, 1:2],
                  127.0, ALU.mult)
            p.op(V, "tensor_scalar", sq_scr[:], xnf[:],
                 stats[:, i, 0:1], m_ap[:], ALU.mult, ALU.add)
            nq = p.op(V, "tensor_single_scalar", xqd[:, i % 2, :], sq_scr[:],
                      m_ap[:], ALU.subtract, sem="MV")
            p.wait(GY, "MV", nq)
            p1_gy[i] = p.dma(GY, xq_scr[i * 128:(i + 1) * 128, :],
                             xqd[:, i % 2, :], "GL")
        p1_mv[i] = p.cnt["MV"]

        kb = (4 + i) if own else (i - 8)
        p.wait(S, "MV", nxnb)
        p1_xb[i] = p.dmat(S, xnT[:, :, kb * 128:(kb + 1) * 128],
                          xnb[:, i % 2, :], "XB")
        mean_pump(3)

    sc_p1.close()

    qn_sb = sbp("qn_sb", [128, 512])
    rop1 = sbp("rop1", [128, 512])
    rop2 = sbp("rop2", [128, 512])
    qhb = sbp("qhb", [128, 2, 512], BF16)
    ssq = sbp("ssq", [128, 4, 4])
    sq2 = sbp("sq2", [128, 128])

    sc_qkv = ExitStack()
    sbq = lambda n, s, d=F32, **kw: sc_qkv.enter_context(
        nc.sbuf_tensor(n, s, d, side="right", **kw))
    khT = sbq("khT", [128, NKV, NKTOK], BF16, is_transpose_dest=True)
    v_sb = sbq("v_sb", [128, 12, 512], BF16)
    qhT = sbq("qhT", [128, NH, NTOK], BF16, is_transpose_dest=True)

    p.dma(SY, ropep[:, 0, :, :], rope4[:, 2, :, :], "LD")
    nk_t = p.dma(SY, ropep[:, 1, :, :], rope4[:, 3, :, :], "LD")
    nq_t = nk_t

    passes = ["k", "v", 0, 1, 2, 3]
    wl = [0] * 6
    pass_mm_end = [0] * 6
    evac_free = {}
    qhb_xb = {}
    tile_ctr = 0

    for pi, pw in enumerate(passes):
        if pi >= 2:
            p.wait(GY, "MM", pass_mm_end[pi - 2])
        if pw == "k":
            srcw = wk.rearrange("(c q) n -> q c n", q=128)
        elif pw == "v":
            srcw = wv.rearrange("(c q) n -> q c n", q=128)
        else:
            srcw = wq[:, pw * 512:(pw + 1) * 512].rearrange(
                "(c q) n -> q c n", q=128)
        wl[pi] = p.dma(GY, wproj[:, pi % 2, :, :], srcw, "GL")

        if pw == 0:
            p.wait(SY, "MV", p.cnt["MV"])
            p.dma(SY, ropep[:, 0, :, :], rope4[:, 0, :, :], "LD")
            nq_t = p.dma(SY, ropep[:, 1, :, :], rope4[:, 1, :, :], "LD")

        tlist = ([4, 5, 6, 7, 8, 9, 10, 11, 0, 1, 2, 3]
                 if isinstance(pw, str) else list(range(8)))
        for tt in tlist:
            bank = tile_ctr % 2
            pb = ps[bank]
            p.wait(TE, "GL", wl[pi])
            key0 = tt if isinstance(pw, str) else (4 + tt)
            i_src = key0 - 4 if key0 >= 4 else key0 + 8
            p.wait(TE, "XB", p1_xb[i_src])
            if bank in evac_free:
                p.wait(TE, evac_free[bank][0], evac_free[bank][1])
            key = tt if isinstance(pw, str) else (4 + tt)
            for mc in range(MC):
                mm(pb[:, 0:512], xnT[:, mc, key * 128:(key + 1) * 128],
                   wproj[:, pi % 2, mc, :], mc == 0, mc == MC - 1,
                   sem="MM" if mc == MC - 1 else None)
            mm_here = p.cnt["MM"]

            if pw == "v":
                p.wait(S, "MM", mm_here)
                nsc = p.op(S, "activation", v_sb[:, tt, :], pb[:, 0:512],
                           AF.Copy, sem="SC")
                evac_free[bank] = ("SC", nsc)
            else:
                rt_idx = tt if pw == "k" else (4 + tt)
                p.wait(S, "MM", mm_here)
                for hh in range(4):
                    nac = p.op(S, "activation", sq2[:],
                               pb[:, hh * 128:(hh + 1) * 128],
                               AF.Square, accum_out=ssq[:, hh, 0:1],
                               sem="SF" if hh == 3 else None)
                p.wait(S, "SF", nac)
                nsc = p.op(S, "activation", ssq[:, :, 1:2], ssq[:, :, 0:1],
                           AF.Sqrt, sem="SC")
                p.wait(V, "SC", nsc)
                p.wait(V, "LD", nk_t if pw == "k" else nq_t)
                p.opf(V, "tensor_scalar_max", ssq[:, :, 2:3], ssq[:, :, 1:2],
                      1e-12)
                p.opf(V, "reciprocal", ssq[:, :, 3:4], ssq[:, :, 2:3])
                qn3 = qn_sb[:].rearrange("q (h x) -> q h x", h=4)
                nqn = p.op(V, "tensor_tensor", qn3,
                           pb[:, 0:512].rearrange("q (h x) -> q h x", h=4),
                           ssq[:, :, 3:4].broadcast_to([128, 4, 128]),
                           ALU.mult, sem="MV")
                p.wait(V, "MV", nqn)
                evac_free[bank] = ("MV", nqn)
                cosb = hb(ropep[:, 0, rt_idx, :], 4, 128)
                p.op(V, "tensor_tensor",
                     rop1[:].rearrange("q (h x) -> q h x", h=4), qn3, cosb,
                     ALU.mult)
                r23 = rop2[:].rearrange("q (h x) -> q h x", h=4)
                p.op(V, "tensor_tensor", r23[:, :, 0:64], qn3[:, :, 64:128],
                     hb(ropep[:, 1, rt_idx, 0:64], 4, 64), ALU.mult)
                p.opf(V, "tensor_tensor", r23[:, :, 64:128], qn3[:, :, 0:64],
                      hb(ropep[:, 1, rt_idx, 64:128], 4, 64), ALU.mult)
                if tile_ctr >= 2 and (tile_ctr - 2) in qhb_xb:
                    p.wait(V, "XB", qhb_xb[tile_ctr - 2])
                nqh = p.op(V, "tensor_tensor", qhb[:, tile_ctr % 2, :], rop1[:],
                           rop2[:], ALU.add, sem="MV")
                p.wait(S, "MV", nqh)
                if pw == "k":
                    dstT = khT[:, :, tt * 128:(tt + 1) * 128]
                else:
                    dstT = qhT[:, pw * 4:(pw + 1) * 4, tt * 128:(tt + 1) * 128]
                qhb_xb[tile_ctr] = p.dmat(S, dstT, qhb[:, tile_ctr % 2, :], "XB")
            tile_ctr += 1
            mean_pump(3)
        pass_mm_end[pi] = p.cnt["MM"]

    while mst["r"] < NCHUNK:
        mean_pump(4)

    p.wait(GY, "MR", m_red[NCHUNK - 1])
    for mi in range(3):
        for m_ in range(12):
            p.dma(GY, s2b[m_:m_ + 1, mi, :], s1b[8 * m_:8 * m_ + 8, mi, :], "GQ")
    p.wait(V, "GQ", p.cnt["GQ"])
    for mi in range(3):
        p.op(V, "tensor_reduce", s2red[:, mi, :], s2b[:, mi, :], AX.X, ALU.add,
             sem="MV")
    p.wait(GY, "MV", p.cnt["MV"])
    for mi in range(3):
        p.dma(GY, fin[0:1, mi, 0:12], s2red[:, mi, :], "GQ")
    p.wait(V, "GQ", p.cnt["GQ"])
    for mi in (0, 2):
        p.op(V, "tensor_reduce", finr[0:1, mi, 0:1], fin[0:1, mi, 0:12],
             AX.X, ALU.add)
    p.opf(V, "tensor_reduce", finr[0:1, 1, 1:3],
          fin[0:1, 1, 0:12].rearrange("a (b c) -> a b c", b=2), AX.X, ALU.add)
    p.opf(V, "tensor_reduce", finr[0:1, 1, 0:1], finr[0:1, 1, 1:3], AX.X,
          ALU.add)
    p.opf(V, "tensor_single_scalar", finr[0:1, 0:3, 1:2], finr[0:1, 0:3, 0:1],
          MEANMUL, ALU.mult)
    p.opf(V, "tensor_scalar_max", finr[0:1, 0:3, 2:3], finr[0:1, 0:3, 1:2],
          1e-5)
    p.opf(V, "reciprocal", finr[0:1, 0:3, 3:4], finr[0:1, 0:3, 2:3])
    ncp = p.op(V, "tensor_copy", fin[0:1, 2, 13:16],
               finr[0:1, 0:3, 3:4].rearrange("a b c -> a (b c)"), sem="MV")
    p.wait(V, "MV", ncp)
    p.wait(TE, "MV", ncp)
    mm(ps[5][:, 0:3], onesf[0:1, :], fin[0:1, 2, 13:16], True, True, sem="MM")
    p.wait(V, "MM", p.cnt["MM"])
    p.opf(V, "tensor_copy", wsc[:, 0:3], ps[5][:, 0:3])
    nw = p.op(V, "reciprocal", wscr[:, 0:3], wsc[:, 0:3], sem="MV")
    p.wait(V, "MV", nw)
    scales_ready = nw

    if DEBUG:
        p.gate([GY])
        p.dma(GY, d_xnT[:], xnT[:].rearrange("q a b -> q (a b)"), "OD")
        p.dma(GY, d_stats[:], stats[:].rearrange("q a b -> q (a b)"), "OD")
        p.dma(GY, d_wsc[:, 0:3], wsc[:], "OD")
        p.dma(GY, d_wsc[:, 3:6], wscr[:], "OD")
        p.dma(GY, d_fin[:], fin[:].rearrange("q a b -> q (a b)"), "OD")
        p.dma(GY, d_s2red[:], s2red[:].rearrange("q a b -> q (a b)"), "OD")
        p.dma(GY, d_xq[:], xq_scr[:], "OD")
        p.gate(ENGINES)
    sc_proj.close()

    sc_mean.close()

    sc_av = ExitStack()
    avn = sc_av.enter_context(nc.sbuf_tensor("avn", [128, 8, NKV, 512], BF16))
    wo_sb = sc_av.enter_context(nc.sbuf_tensor("wo_sb", [128, 4, MC, 512],
                                               BF16))
    sc_a3 = ExitStack()
    sba = lambda n, s, d=F32, **kw: sc_a3.enter_context(
        nc.sbuf_tensor(n, s, d, **kw))
    mask_sb = sba("mask_sb", [128, 8, 5, 128], BF16)
    p_bf = sba("p_bf", [128, 2, 512], BF16)
    den_s = sba("den_s", [1, 512])
    rden = sba("rden", [1, 512])
    rdb = sba("rdb", [128, 512])

    p.wait(GY, "MV", scales_ready)
    wol = 0
    for mcq in range(4):
        wol = p.dma(GY, wo_sb[:, mcq, :, :],
                    wo[:, mcq * 512:(mcq + 1) * 512]
                    .rearrange("(c q) n -> q c n", q=128), "GL")

    p.wait(SY, "MM", p.cnt["MM"])
    p.wait(SY, "MR", p.cnt["MR"])
    p.wait(SY, "MV", scales_ready)
    nmask = p.dma(SY, mask_sb[:],
                  masks_i.rearrange("q (a b c) -> q a b c", a=8, b=5), "LD")
    p.wait(TE, "LD", nmask)
    p.wait(TE, "XB", p.cnt["XB"])

    av_free = {}
    sb_free = {}
    pbf_mm = {}
    for kv in range(4):
        for qt in range(8):
            it = kv * 8 + qt
            abank = 6 + (it % 2)
            psa = ps[abank]
            for r in range(5):
                kb = qt + r
                sbank = 2 + (r % 2)
                pss = ps[sbank]
                if sbank in sb_free:
                    p.wait(TE, sb_free[sbank][0], sb_free[sbank][1])
                if r == 0 and abank in av_free:
                    p.wait(TE, av_free[abank][0], av_free[abank][1])
                mm(pss[:, 0:512], khT[:, kv, kb * 128:(kb + 1) * 128],
                   qhT[:, kv * 4:(kv + 1) * 4, qt * 128:(qt + 1) * 128],
                   True, True, sem="MM")
                mm_s = p.cnt["MM"]
                p.wait(S, "MM", mm_s)
                if (it, r - 2) in pbf_mm:
                    p.wait(S, "MM", pbf_mm[(it, r - 2)])
                elif r < 2 and (it - 1, r + 3) in pbf_mm:
                    p.wait(S, "MM", pbf_mm[(it - 1, r + 3)])
                nexp = p.op(S, "activation", p_bf[:, r % 2, :], pss[:, 0:512],
                            AF.Exp, sem="SC")
                sb_free[sbank] = ("SC", nexp)
                p.wait(V, "SC", nexp)
                p4 = p_bf[:, r % 2, :].rearrange("q (h x) -> q h x", h=4)
                nmsk = p.op(V, "tensor_tensor", p4, p4,
                            hb(mask_sb[:, qt, r, :], 4, 128), ALU.mult, sem="MV")
                p.wait(TE, "MV", nmsk)
                mm(ps[4][0:1, 0:512], onesb[:, 0:1], p_bf[:, r % 2, :],
                   r == 0, r == 4)
                pbf_mm[(it, r)] = mm(psa[:, 0:512],
                                     v_sb[:, kb, kv * 128:(kv + 1) * 128],
                                     p_bf[:, r % 2, :], r == 0, r == 4, sem="MM")
            p.wait(V, "MM", p.cnt["MM"])
            p.opf(V, "tensor_copy", den_s[:], ps[4][0:1, 0:512])
            nrd = p.op(V, "reciprocal", rden[:], den_s[:], sem="MV")
            p.wait(TE, "MV", nrd)
            mm(ps[5][:, 0:512], onesf[0:1, :], rden[0:1, :], True, True,
               sem="MM")
            p.wait(S, "MM", p.cnt["MM"])
            nrdb = p.op(S, "activation", rdb[:], ps[5][:, 0:512], AF.Copy,
                        sem="SC")
            p.wait(V, "SC", nrdb)
            nav = p.op(V, "tensor_tensor", avn[:, qt, kv, :], psa[:, 0:512],
                       rdb[:], ALU.mult, sem="MV")
            av_free[abank] = ("MV", nav)

    if DEBUG:
        p.gate([GY])
        p.dma(GY, d_qhT[:], qhT[:].rearrange("q a b -> q (a b)"), "OD")
        p.dma(GY, d_khT[:], khT[:].rearrange("q a b -> q (a b)"), "OD")
        p.dma(GY, d_v[:], v_sb[:].rearrange("q a b -> q (a b)"), "OD")
        p.dma(GY, d_avn[:], avn[:].rearrange("q a b c -> q (a b c)"), "OD")
        p.gate(ENGINES)
    sc_qkv.close()
    sc_a3.close()

    a3_mm = p.cnt["MM"]
    a3_mv = p.cnt["MV"]
    sc_xq = ExitStack()
    xqT = sc_xq.enter_context(nc.sbuf_tensor("xqT", [128, MC, NTOK], BF16,
                                             side="right",
                                             is_transpose_dest=True))
    sc_a4 = ExitStack()
    xat = sc_a4.enter_context(nc.sbuf_tensor("xat", [128, 2, HID], F32))
    xq_rl = sc_a4.enter_context(nc.sbuf_tensor("xq_rl", [128, 2, HID], BF16))

    p.gate([SY, GY])
    avn_ready = p.cnt["MV"]
    p.wait(TE, "MV", avn_ready)
    p.wait(S, "MM", a3_mm)
    p.wait(S, "MV", a3_mv)
    xq_xb = [0] * 8
    a4_ev = [0] * 8
    a4_sp = [0] * 8
    for qt in range(8):
        if qt >= 2:
            p.wait(SY, "HS", a4_sp[qt - 2])
            p.wait(SY, "XB", xq_xb[qt - 2])
        xld = p.dma(SY, xat[:, qt % 2, :], x_tiled[qt], "LD")
        nxq = p.dma(SY, xq_rl[:, qt % 2, :],
                    xq_scr[qt * 128:(qt + 1) * 128, :], "FL")
        p.wait(S, "FL", nxq)
        xq_xb[qt] = p.dmat(S, xqT[:, :, qt * 128:(qt + 1) * 128],
                           xq_rl[:, qt % 2, :], "XB")
        p.wait(TE, "GL", wol)
        for mcq in range(4):
            bank = (qt * 4 + mcq) % 2
            pb = ps[bank]
            if qt >= 1:
                p.wait(TE, "MV", a4_ev[qt - 1])
            for h in range(16):
                kvi, hh = divmod(h, 4)
                mm(pb[:, 0:512], avn[:, qt, kvi, hh * 128:(hh + 1) * 128],
                   wo_sb[:, mcq, h, :], h == 0, h == 15,
                   sem="MM" if h == 15 else None)
            mmw = p.cnt["MM"]
            p.wait(V, "MM", mmw)
            if mcq == 0:
                p.wait(V, "LD", xld)
                if qt >= 2:
                    p.wait(V, "HS", a4_sp[qt - 2])
            nev = p.op(V, "tensor_tensor",
                       xat[:, qt % 2, mcq * 512:(mcq + 1) * 512],
                       xat[:, qt % 2, mcq * 512:(mcq + 1) * 512],
                       pb[:, 0:512], ALU.add, sem="MV")
        a4_ev[qt] = p.cnt["MV"]
        p.wait(GY, "MV", a4_ev[qt])
        a4_sp[qt] = p.dma(GY, xatt_scr[qt * 128:(qt + 1) * 128, :],
                          xat[:, qt % 2, :], "HS")
    if DEBUG:
        p.gate([GY])
        p.dma(GY, d_xatt[:], xatt_scr[:], "OD")
        p.gate(ENGINES)
    sc_a4.close()
    sc_av.close()

    sc_f1 = ExitStack()
    sbf = lambda n, s, d=F32, **kw: sc_f1.enter_context(
        nc.sbuf_tensor(n, s, d, **kw))
    wfstage = sbf("wfstage", [128, 2, 8, 512])
    absw = sbf("absw", [128, 2, 8, 512])
    sgnw = sbf("sgnw", [128, 2, 8, 512])
    t1g = sbf("t1g", [128, MC, 512], BF16)
    t3g = sbf("t3g", [128, MC, 512], BF16)
    t2b = sbf("t2b", [128, 2, HID], BF16)
    silu_s = sbf("silu_s", [128, 2, 512])
    h_sb = sbf("h_sb", [128, 2, 512])

    xb_f0 = p.cnt["XB"]
    p.gate([SY, GY])
    p.wait(V, "MV", scales_ready)
    p.opf(V, "reciprocal", stats[:, 0:8, 1:2], stats[:, 0:8, 0:1])
    p.op(V, "tensor_tensor", stats[:, 0:8, 2:3], stats[:, 0:8, 1:2],
         wscr[:, 0:1].unsqueeze(1).broadcast_to([128, 8, 1]), ALU.mult)
    p.op(V, "tensor_tensor", stats[:, 0:8, 3:4], stats[:, 0:8, 1:2],
         wscr[:, 2:3].unsqueeze(1).broadcast_to([128, 8, 1]), ALU.mult,
         sem="MV")

    f1_ev = {}
    f1_hs = {}
    f1_mm_end = [0] * 11
    s_done = {}
    sttn = {}
    t2hs = {}

    def f1_chunk(g, c):
        fc = 6 * g + c
        buf = fc % 2
        wff = wfstage[:, buf].rearrange("q c n -> q (c n)")
        absv = absw[:, buf].rearrange("q c n -> q (c n)")
        sgnv = sgnw[:, buf].rearrange("q c n -> q (c n)")
        eng = SY if c in (0, 1, 4) else GY
        if c < 4:
            wsrc = w1t if c < 2 else w3t
            hf = c % 2
            src = wsrc[hf * 1024:(hf + 1) * 1024, g * 512:(g + 1) * 512] \
                .rearrange("(c q) n -> q c n", q=128)
            dst_ap = wfstage[:, buf]
            wcol = 0 if c < 2 else 2
        else:
            pc = 2 * g + (c - 4)
            src = w2t[pc * 256:(pc + 1) * 256, :] \
                .rearrange("(c q) n -> q c n", q=128)
            dst_ap = wfstage[:, buf].rearrange("q c n -> q (c n)") \
                .rearrange("q (c n) -> q c n", c=2)
            wcol = 1
        lsem = "FL" if eng == SY else "FG"
        if fc >= 2:
            p.wait(eng, "SC", s_done[fc - 2])
        nwl = p.dma(eng, dst_ap, src, lsem)
        p.wait(S, lsem, nwl)
        if fc == 0:
            p.wait(S, "MV", scales_ready)
        if fc >= 2:
            p.wait(S, "MV", sttn[fc - 2])
        p.op(S, "activation", absv, wff, AF.Abs, scale=wsc[:, wcol:wcol + 1])
        s_done[fc] = p.op(S, "activation", sgnv, wff, AF.Sign, sem="SC")
        p.wait(V, "SC", s_done[fc])
        if c < 4:
            if c == 0 and g >= 1:
                p.wait(V, "MM", f1_mm_end[g - 1])
            tdst = t1g if c < 2 else t3g
            hf = c % 2
            tdf = tdst[:, hf * 8:(hf + 1) * 8, :].rearrange("q c n -> q (c n)")
            sttn[fc] = p.op(V, "scalar_tensor_tensor", tdf, absv, 0.5, sgnv,
                            ALU.is_gt, ALU.mult, sem="MV")
        else:
            pc = 2 * g + (c - 4)
            if pc >= 1:
                p.wait(V, "HS", t2hs[pc - 1])
            t2f = t2b[:].rearrange("q a b -> q (a b)")
            sttn[fc] = p.op(V, "scalar_tensor_tensor", t2f, absv, 0.5, sgnv,
                            ALU.is_gt, ALU.mult, sem="MV")
            p.wait(GY, "MV", sttn[fc])
            for j2 in range(2):
                t2hs[pc] = p.dma(
                    GY, t2_scr[(2 * pc + j2) * 128:(2 * pc + j2 + 1) * 128, :],
                    t2b[:, j2, :], "HS")

    for g in range(11):
        for c in range(4):
            f1_chunk(g, c)
        p.wait(TE, "MV", sttn[6 * g + 3])
        p.wait(TE, "XB", xb_f0)
        for tt in range(8):
            it = g * 8 + tt
            ba, bb = 2 * (it % 2), 2 * (it % 2) + 1
            for b_ in (ba, bb):
                if b_ in f1_ev:
                    p.wait(TE, f1_ev[b_][0], f1_ev[b_][1])
            for mc in range(MC):
                lhs = xqT[:, mc, tt * 128:(tt + 1) * 128]
                mm(ps[ba][:, 0:512], lhs, t1g[:, mc, :], mc == 0, mc == MC - 1)
                mm(ps[bb][:, 0:512], lhs, t3g[:, mc, :], mc == 0, mc == MC - 1,
                   sem="MM" if mc == MC - 1 else None)
            mmab = p.cnt["MM"]
            p.wait(S, "MM", mmab)
            nsl = p.op(S, "activation", silu_s[:, it % 2, :], ps[ba][:, 0:512],
                       AF.Silu, scale=stats[:, tt, 2:3], sem="SC")
            p.wait(V, "SC", nsl)
            if it >= 2 and (it - 2) in f1_hs:
                p.wait(V, "HS", f1_hs[it - 2])
            p.op(V, "scalar_tensor_tensor", h_sb[:, it % 2, :], ps[bb][:, 0:512],
                 stats[:, tt, 3:4], silu_s[:, it % 2, :], ALU.mult, ALU.mult)
            f1_ev[ba] = ("SC", nsl)
            nhm = p.op(V, "reduce_max", hpart[:, tt, g:g + 1],
                       h_sb[:, it % 2, :], AX.X, apply_absolute_value=True,
                       sem="MV")
            f1_ev[bb] = ("MV", nhm)
            p.wait(GY, "MV", nhm)
            f1_hs[it] = p.dma(GY, h_scr[tt * 128:(tt + 1) * 128,
                                        g * 512:(g + 1) * 512],
                              h_sb[:, it % 2, :], "HS")
            if tt == 1:
                f1_chunk(g, 4)
            elif tt == 3:
                f1_chunk(g, 5)
        f1_mm_end[g] = p.cnt["MM"]
    if DEBUG:
        p.gate([GY])
        p.dma(GY, d_hscr[:], h_scr[:], "OD")
        p.dma(GY, d_t2[:], t2_scr[:], "OD")
        p.dma(GY, d_t1g[:], t1g[:].rearrange("q a b -> q (a b)"), "OD")
        p.dma(GY, d_t3g[:], t3g[:].rearrange("q a b -> q (a b)"), "OD")
        p.dma(GY, d_xqT[:], xqT[:].rearrange("q a b -> q (a b)"), "OD")
        p.gate(ENGINES)
    sc_f1.close()
    sc_xq.close()

    sc_f2 = ExitStack()
    sbf2 = lambda n, s, d=F32, **kw: sc_f2.enter_context(
        nc.sbuf_tensor(n, s, d, **kw))
    hbig = sbf2("hbig", [128, 2, 1408])
    hq_bf = sbf2("hq_bf", [128, 2, 1408], BF16)
    hqT = sbf2("hqT", [128, HC, 512], BF16, is_transpose_dest=True)
    t2g = sbf2("t2g", [128, 2, 8, HID], BF16)
    xa_half = sbf2("xa_half", [128, 4, HID])

    p.gate([SY, GY])
    GRP = [8, 8, 8, 8, 8, 4]
    f3_ev = {}
    f3_mm_end = {}
    hq_xb = {}
    for th in range(2):
        p.wait(S, "MM", p.cnt["MM"])
        p.wait(SY, "MM", p.cnt["MM"])
        nxa = p.dma(SY, xa_half[:],
                    xatt_scr[th * 512:(th + 1) * 512, :]
                    .rearrange("(t q) h -> q t h", q=128), "HL")
        T4 = slice(th * 4, th * 4 + 4)
        p.opf(V, "reduce_max", hamax[:, T4, 0:1], hpart[:, T4, 0:11], AX.X)
        p.opf(V, "tensor_scalar_max", hamax[:, T4, 0:1], hamax[:, T4, 0:1],
              1e-5)
        p.opf(V, "reciprocal", hamax[:, T4, 1:2], hamax[:, T4, 0:1])
        p.opf(V, "tensor_single_scalar", hamax[:, T4, 1:2],
              hamax[:, T4, 1:2], 127.0, ALU.mult)
        p.opf(V, "reciprocal", hamax[:, T4, 2:3], hamax[:, T4, 1:2])
        p.opf(V, "tensor_tensor", hamax[:, T4, 3:4], hamax[:, T4, 2:3],
              wscr[:, 1:2].unsqueeze(1).broadcast_to([128, 4, 1]), ALU.mult)
        for tl in range(4):
            tt = th * 4 + tl
            for qr in range(4):
                idx = tl * 4 + qr
                if idx >= 2:
                    p.wait(SY, "XB", hq_xb.get((th, idx - 2), 0))
                nhl = p.dma(SY, hbig[:, idx % 2, :],
                            h_scr[tt * 128:(tt + 1) * 128,
                                  qr * 1408:(qr + 1) * 1408], "HL")
                p.wait(V, "HL", nhl)
                if idx >= 2:
                    p.wait(V, "XB", hq_xb.get((th, idx - 2), 0))
                p.op(V, "tensor_scalar", hbig[:, idx % 2, :], hbig[:, idx % 2, :],
                     hamax[:, tt, 1:2], m_ap[:], ALU.mult, ALU.add)
                nhq = p.op(V, "tensor_single_scalar", hq_bf[:, idx % 2, :],
                           hbig[:, idx % 2, :], m_ap[:], ALU.subtract, sem="MV")
                p.wait(S, "MV", nhq)
                hq_xb[(th, idx)] = p.dmat(
                    S, hqT[:, qr * 11:(qr + 1) * 11, tl * 128:(tl + 1) * 128],
                    hq_bf[:, idx % 2, :], "XB")
        p.wait(TE, "XB", p.cnt["XB"])
        if DEBUG:
            p.wait(GY, "XB", p.cnt["XB"])
            p.dma(GY, d_hqT[:, th * HC * 512:(th + 1) * HC * 512],
                  hqT[:].rearrange("q a b -> q (a b)"), "OD")
            p.wait(S, "OD", p.cnt["OD"])
        p.wait(V, "HL", nxa)

        hc0 = 0
        for gi, gn in enumerate(GRP):
            gkey = th * 6 + gi
            if gkey >= 2:
                p.wait(SY, "MM", f3_mm_end[gkey - 2])
            ntg = p.dma(SY, t2g[:, gi % 2, 0:gn, :],
                        t2_scr[hc0 * 128:(hc0 + gn) * 128, :]
                        .rearrange("(c q) n -> q c n", q=128), "TG")
            p.wait(TE, "TG", ntg)
            if gkey == 0:
                p.wait(TE, "HS", p.cnt["HS"])
            for tl in range(4):
                tt = th * 4 + tl
                for mcq in range(4):
                    bank = (tl * 4 + mcq) % 2
                    pb = ps[bank]
                    if bank in f3_ev:
                        p.wait(TE, f3_ev[bank][0], f3_ev[bank][1])
                    for j in range(gn):
                        mm(pb[:, 0:512],
                           hqT[:, hc0 + j, tl * 128:(tl + 1) * 128],
                           t2g[:, gi % 2, j, mcq * 512:(mcq + 1) * 512],
                           j == 0, j == gn - 1,
                           sem="MM" if j == gn - 1 else None)
                    mmf = p.cnt["MM"]
                    p.wait(V, "MM", mmf)
                    nev = p.op(V, "scalar_tensor_tensor",
                               xa_half[:, tl, mcq * 512:(mcq + 1) * 512],
                               pb[:, 0:512], hamax[:, tt, 3:4],
                               xa_half[:, tl, mcq * 512:(mcq + 1) * 512],
                               ALU.mult, ALU.add, sem="MV")
                    f3_ev[bank] = ("MV", nev)
            f3_mm_end[gkey] = p.cnt["MM"]
            hc0 += gn

        p.wait(GY, "MV", p.cnt["MV"])
        for tl in range(4):
            p.dma(GY, out_tiled[th * 4 + tl], xa_half[:, tl, :], "OD")
        p.wait(SY, "OD", p.cnt["OD"])

    if DEBUG:
        p.gate([GY])
        p.dma(GY, d_hamax[:], hamax[:].rearrange("q a b -> q (a b)"), "OD")
    p.wait(GY, "OD", p.cnt["OD"])
    sc_f2.close()

    blk = ctx.enter_context(nc.Block())
    for ename, dec in (("sync", blk.sync), ("vector", blk.vector),
                       ("scalar", blk.scalar), ("tensor", blk.tensor),
                       ("gpsimd", blk.gpsimd)):
        def body(e, ename=ename):
            for fn in p.q[ename]:
                fn(e)
        dec(body)
    ctx.close()
    return nc



def build_chains(w):
    w = np.asarray(w, np.float32)
    R, C = w.shape
    r1, c1 = R // 32, C // 32
    wr = w.reshape(r1, 32, c1, 32).transpose(0, 2, 1, 3).reshape(r1, c1, 1024)
    outa = np.zeros((96, 128, 1024), np.float32)
    pr, pc = (-r1) % 32, (-c1) % 32
    r2, c2 = (r1 + pr) // 32, (c1 + pc) // 32
    for k in range(r2):
        for l in range(c2):
            m = k * c2 + l
            for rr in range(32):
                a = 32 * k + rr - pr // 2
                if a < 0 or a >= r1:
                    continue
                for cc in range(32):
                    b_ = 32 * l + cc - pc // 2
                    if b_ < 0 or b_ >= c1:
                        continue
                    pos = rr * 32 + cc
                    outa[8 * m + pos // 128, pos % 128] = wr[a, b_]
    return outa.reshape(96, 131072)


def build_masks(s0):
    kk = np.arange(128)[:, None]
    qq = np.arange(128)[None, :]
    m = np.zeros((128, 8, 5, 128), np.float32)
    for qt in range(8):
        for r in range(5):
            if (s0 - 512) // 128 + qt + r < 0:
                continue
            if r == 0:
                m[:, qt, r, :] = (kk > qq)
            elif r == 4:
                m[:, qt, r, :] = (kk <= qq)
            else:
                m[:, qt, r, :] = 1.0
    return np.ascontiguousarray(m.reshape(128, 40 * 128)).astype(BF)


def build_ropes(s0):
    inv_freq = (1.0 / (10000.0 ** (np.arange(0, D, 2, dtype=np.float32)
                                   / np.float32(D)))).astype(np.float32)
    pos = np.maximum(np.arange(s0 - 512, s0 + 1024, dtype=np.float32), 0.0)
    fr = pos[:, None] * inv_freq[None, :]
    cos = np.concatenate([np.cos(fr), np.cos(fr)], -1).astype(np.float32)
    sin = np.concatenate([np.sin(fr), np.sin(fr)], -1).astype(np.float32)
    sinf = sin.copy()
    sinf[:, :64] *= np.float32(-1.0)
    sc = np.float32(SCALE)
    tabs = [(cos * sc).astype(np.float32), (sinf * sc).astype(np.float32),
            cos, sinf]
    lay = [t.reshape(12, 128, 128).transpose(1, 0, 2) for t in tabs]
    return np.ascontiguousarray(np.stack(lay, 1)).reshape(128, 4 * 12 * 128)


_NC_CACHE = None


def kernel(L0_attn_x, L0_attn_wq, L0_attn_wk, L0_attn_wv, L0_attn_wo,
           L0_ffn_w1, L0_ffn_w2, L0_ffn_w3):
    global _NC_CACHE
    x = np.ascontiguousarray(np.asarray(L0_attn_x, np.float32))
    wq = np.ascontiguousarray(np.asarray(L0_attn_wq, np.float32))
    wk = np.ascontiguousarray(np.asarray(L0_attn_wk, np.float32))
    wv = np.ascontiguousarray(np.asarray(L0_attn_wv, np.float32))
    wo = np.ascontiguousarray(np.asarray(L0_attn_wo, np.float32))
    w1 = np.asarray(L0_ffn_w1, np.float32)
    w2 = np.asarray(L0_ffn_w2, np.float32)
    w3 = np.asarray(L0_ffn_w3, np.float32)

    w1t = np.ascontiguousarray(w1.T)
    w3t = np.ascontiguousarray(w3.T)
    w2t = np.ascontiguousarray(w2.T)
    w1c, w2c, w3c = build_chains(w1), build_chains(w2), build_chains(w3)

    consts = np.zeros((128, 16), np.float32)
    consts[:, 0] = 16777216.0
    consts[:, 4] = 1e-5
    ones_bf = np.ones((128, 1), BF)
    ones_f1 = np.ones((1, 128), np.float32)

    if _NC_CACHE is None:
        _NC_CACHE = build_nc()
    nc = _NC_CACHE

    in_maps = []
    for c in range(8):
        b, half = divmod(c, 2)
        s0 = half * 1024
        xo = np.ascontiguousarray(x[b, s0:s0 + 1024, :])
        xh = (np.ascontiguousarray(x[b, s0 - 512:s0, :]) if s0 > 0
              else np.zeros((512, HID), np.float32))
        in_maps.append({
            "x_own": xo, "x_halo": xh, "wq": wq, "wk": wk, "wv": wv, "wo": wo,
            "w1t": w1t, "w3t": w3t, "w2t": w2t,
            "w1c": w1c, "w2c": w2c, "w3c": w3c,
            "masks": build_masks(s0), "ropes": build_ropes(s0),
            "consts": consts, "ones_bf": ones_bf, "ones_f1": ones_f1,
        })

    res = run_bass_kernel_spmd(nc, in_maps, core_ids=list(range(8)))
    outp = np.empty((4, 2048, HID), np.float32)
    for c in range(8):
        b, half = divmod(c, 2)
        s0 = half * 1024
        outp[b, s0:s0 + 1024, :] = np.asarray(res.results[c]["out"])
    return outp
